# version 06 (final)
# baseline (speedup 1.0000x reference)
import numpy as np
import ml_dtypes
from contextlib import ExitStack

import concourse.bass as bass
import concourse.mybir as mybir
from concourse.bass_utils import run_bass_kernel_spmd
from concourse import library_config as LC

F32 = mybir.dt.float32
BF16 = mybir.dt.bfloat16
I16 = mybir.dt.int16
U32 = mybir.dt.uint32
ALU = mybir.AluOpType
AF = mybir.ActivationFunctionType
AXL = mybir.AxisListType

B, T, D = 16, 2048, 1024
NCORES = 8
BPC = B // NCORES
NTOK = BPC * T
S = 64
H, DH = 16, 64
NE = 8
DFF = 4096
CAP = 512
KCV = 256
CCH = 2 * D
LN_EPS = 1e-5
CLAMP = 1000000.0
NT512 = NTOK // 512
NCH = D // 128
NSLOT = 24
NSL = NSLOT * 128

PERM = np.concatenate([np.arange(0, D, 2), np.arange(1, D, 2)])
IPERM = np.argsort(PERM)
BF = ml_dtypes.bfloat16

PHASE_LIMIT = 99


def _pair(w):
    w = np.ascontiguousarray(w, dtype=np.float32)
    hi = w.astype(BF)
    lo = (w - hi.astype(np.float32)).astype(BF)
    return np.ascontiguousarray(hi), np.ascontiguousarray(lo)


class Sem:
    __slots__ = ("h", "count")

    def __init__(self, h):
        self.h = h
        self.count = 0


class KB:
    def __init__(self):
        self.nc = bass.Bass("TRN2", target_bir_lowering=False, debug=False)
        self.es = ExitStack()
        nc = self.nc
        self.t, self.v, self.sc, self.gp, self.sy = (
            nc.tensor, nc.vector, nc.scalar, nc.gpsimd, nc.sync)
        self.engines = [self.t, self.v, self.sc, self.gp, self.sy]
        self.sems = {}
        self.inputs = {}
        self.dram = {}

    def sem(self, name):
        if name not in self.sems:
            self.sems[name] = Sem(self.es.enter_context(self.nc.semaphore(name)))
        return self.sems[name]

    def mark(self, instr, name, n=1):
        s = self.sem(name)
        instr.then_inc(s.h, n)
        s.count += n
        return instr

    def need(self, eng, name):
        s = self.sem(name)
        if s.count:
            eng.wait_ge(s.h, s.count)

    def sbuf(self, name, shape, dt, es=None):
        return (es or self.es).enter_context(self.nc.sbuf_tensor(name, list(shape), dt))

    def dram_in(self, name, arr, dt=None):
        arr = np.ascontiguousarray(arr)
        if dt is None:
            dt = {np.dtype(np.float32): F32, np.dtype(BF): BF16,
                  np.dtype(np.int16): I16, np.dtype(np.uint32): U32}[arr.dtype]
        h = self.nc.dram_tensor(name, list(arr.shape), dt, kind="ExternalInput")
        self.inputs[name] = arr
        self.dram[name] = h
        return h.ap()

    def dram_scratch(self, name, shape, dt):
        h = self.nc.dram_tensor(name, list(shape), dt, kind="Internal")
        self.dram[name] = h
        return h.ap()

    def dram_out(self, name, shape, dt):
        h = self.nc.dram_tensor(name, list(shape), dt, kind="ExternalOutput")
        self.dram[name] = h
        return h.ap()

    def tile_sync(self, engines=None):
        for e in (engines or self.engines):
            for s2 in self.sems.values():
                if s2.count:
                    e.wait_ge(s2.h, s2.count)

    def barrier(self):
        s = self.sem("bar")
        for e in self.engines:
            e.nop().then_inc(s.h, 1)
        s.count += 5
        for e in self.engines:
            e.wait_ge(s.h, s.count)
            for s2 in self.sems.values():
                if s2.count:
                    e.wait_ge(s2.h, s2.count)

    def mm3(self, out, l_hi, l_lo, r_hi, r_lo, start, stop, mark=None):
        t = self.t
        t.matmul(out, l_hi, r_hi, start=start, stop=False)
        t.matmul(out, l_hi, r_lo, start=False, stop=False)
        last = t.matmul(out, l_lo, r_hi, start=False, stop=stop)
        if mark:
            self.mark(last, mark)
        return last


def build_program(W):
    kb = KB()
    nc = kb.nc
    P = PERM
    t, v, sc, gp, sy = kb.t, kb.v, kb.sc, kb.gp, kb.sy

    def pairs_in(name, arr):
        hi, lo = _pair(arr)
        return (kb.dram_in(name + "_hi", hi), kb.dram_in(name + "_lo", lo))

    x_in = kb.dram_in("x", np.zeros((D, NTOK), np.float32))
    dA = pairs_in("A", np.asarray(W["L0_attn_A"]))
    dBmT = pairs_in("BmT", np.asarray(W["L0_attn_Bm"])[:, P].T)
    dCT = pairs_in("CT", np.asarray(W["L0_attn_C"])[P, :].T)
    d_sgb = kb.dram_in("ssm_gb", np.stack(
        [np.asarray(W["L0_attn_gamma"])[P], np.asarray(W["L0_attn_beta"])[P]], 1).astype(np.float32))

    ipw = np.asarray(W["L0_ffn_in_proj_w"]); ipb = np.asarray(W["L0_ffn_in_proj_b"])
    dWq = pairs_in("Wq", ipw[:D][:, P].T)
    dWk = pairs_in("Wk", ipw[D:2 * D][:, P].T)
    dWv = pairs_in("Wv", ipw[2 * D:][:, P].T)
    dqkvb = kb.dram_in("qkvb", np.stack(
        [ipb[:D], ipb[D:2 * D], ipb[2 * D:]], 1).astype(np.float32))
    dWo = pairs_in("Wo", np.asarray(W["L0_ffn_out_proj_w"])[P, :].T)
    dlnb0 = kb.dram_in("lnb0", np.stack(
        [np.asarray(W["L0_ffn_out_proj_b"])[P],
         np.asarray(W["L0_ffn_ln1_w"])[P], np.asarray(W["L0_ffn_ln1_b"])[P],
         np.asarray(W["L0_ffn_ln2_w"])[P], np.asarray(W["L0_ffn_ln2_b"])[P]], 1).astype(np.float32))
    dgw0 = pairs_in("gw0", np.asarray(W["L0_ffn_gate_w"])[:, P].T)
    dgb0 = kb.dram_in("gb0", np.asarray(W["L0_ffn_gate_b"]).reshape(1, NE).astype(np.float32))
    ew = np.asarray(W["L0_ffn_expert_w"])
    dEW = pairs_in("EW", np.ascontiguousarray(np.stack([ew[e][P][:, P].T for e in range(NE)])))
    dEB = kb.dram_in("EB", np.ascontiguousarray(
        np.asarray(W["L0_ffn_expert_b"])[:, P].T.astype(np.float32)))

    dWin = pairs_in("Win", np.asarray(W["L1_attn_w_in"])[:, P].T)
    dbin = kb.dram_in("bin", np.ascontiguousarray(
        np.asarray(W["L1_attn_b_in"]).reshape(CCH // 128, 128).T.astype(np.float32)))
    wa = np.asarray(W["L1_attn_w_a"], dtype=np.float32).reshape(D, KCV)
    wb = np.asarray(W["L1_attn_w_b"], dtype=np.float32).reshape(D, KCV)
    wext = np.zeros((CCH, 512), np.float32)
    wext[:, 128:384] = np.concatenate([wa, wb], axis=0)
    dwext = pairs_in("wext", wext)
    dbar = kb.dram_in("bar_row", np.concatenate(
        [np.asarray(W["L1_attn_b_a"]), np.asarray(W["L1_attn_b_b"])]).astype(np.float32).reshape(1, CCH))
    dWo1 = pairs_in("Wo1", np.asarray(W["L1_attn_w_out"])[P, :].T)
    dbo1 = kb.dram_in("bo1", np.asarray(W["L1_attn_b_out"])[P].reshape(D, 1).astype(np.float32))

    dgw1 = pairs_in("gw1", np.asarray(W["L1_ffn_gate_w"])[P, :])
    dW1 = pairs_in("W1", np.ascontiguousarray(np.asarray(W["L1_ffn_w1"])[:, P, :]))
    dW2 = pairs_in("W2", np.ascontiguousarray(np.asarray(W["L1_ffn_w2"])[:, :, P]))

    inv_freq = (1.0 / (10000.0 ** (np.arange(0, DH, 2, dtype=np.float32) / np.float32(DH)))).astype(np.float32)
    freqs = (np.arange(T, dtype=np.float32)[:, None] * inv_freq[None, :]).astype(np.float32)
    drsin = kb.dram_in("rsin", np.ascontiguousarray(np.sin(freqs).astype(np.float32).T[np.arange(128) % 32]))
    drcos = kb.dram_in("rcos", np.ascontiguousarray(np.cos(freqs).astype(np.float32).T[np.arange(128) % 32]))
    dident = kb.dram_in("identb", np.eye(128, dtype=np.float32).astype(BF))
    didentf = kb.dram_in("identf", np.eye(128, dtype=np.float32))
    drevf = kb.dram_in("revf", np.ascontiguousarray(np.eye(128, dtype=np.float32)[:, ::-1]))
    donesc = kb.dram_in("onesc", np.ones((128, 1), BF))
    donesr = kb.dram_in("onesr", np.ones((1, 2048), BF))
    dzeror = kb.dram_in("zeror", np.zeros((1, 2048), BF))
    msk = np.zeros((128, 4 * 512), np.float32)
    qq_ = np.arange(512)[None, :]; kk_ = np.arange(128)[:, None]
    for j in range(4):
        msk[:, j * 512:(j + 1) * 512] = ((qq_ - 128 * j) >= kk_).astype(np.float32)
    dmask4 = kb.dram_in("mask4", msk.astype(BF))
    dtri16 = kb.dram_in("tri16", np.triu(np.ones((16, 16), np.float32), 1).astype(BF))
    diota16 = kb.dram_in("iota16", np.ascontiguousarray(
        np.arange(T, dtype=np.float32).reshape(T // 16, 16).T))

    ds = kb.dram_scratch
    xn32d = ds("xn32", [D, NTOK], F32)
    xnH = ds("xnH", [D, NTOK], BF16); xnL = ds("xnL", [D, NTOK], BF16)
    rHd = ds("rH", [D, NTOK], BF16); rLd = ds("rL", [D, NTOK], BF16)
    qHd = ds("qH", [D, NTOK], BF16); qLd = ds("qL", [D, NTOK], BF16)
    kHd = ds("kH", [D, NTOK], BF16); kLd = ds("kL", [D, NTOK], BF16)
    vHd = ds("vH", [D, NTOK], BF16); vLd = ds("vL", [D, NTOK], BF16)
    cxHd = ds("cxH", [D, NTOK], BF16); cxLd = ds("cxL", [D, NTOK], BF16)
    x1d = ds("x1_32", [D, NTOK], F32)
    x1H = ds("x1H", [D, NTOK], BF16); x1L = ds("x1L", [D, NTOK], BF16)
    s1d = ds("s1_32", [D, NTOK], F32)
    g0d = ds("g0_32", [D, NTOK], F32)
    xBd = ds("xB32", [D, NTOK], F32)
    xn2H = ds("xn2H", [D, NTOK], BF16); xn2L = ds("xn2L", [D, NTOK], BF16)
    xt2H = ds("xt2H", [BPC, T, D], BF16); xt2L = ds("xt2L", [BPC, T, D], BF16)
    zcmH = ds("zcmH", [CCH, BPC * NSL], BF16)
    zcmL = ds("zcmL", [CCH, BPC * NSL], BF16)
    gatH = ds("gatH", [CCH, NTOK], BF16); gatL = ds("gatL", [CCH, NTOK], BF16)
    s2d = ds("s2_32", [D, NTOK], F32)
    acc1 = ds("acc1", [BPC, 2, T, D], F32)
    pbd = ds("pbounce", [16, CAP // 16], F32)
    out_x = kb.dram_out("out", [D, NTOK], F32)

    PSUM = kb.es.enter_context(nc.psum_tensor("psA", [128, 4096], F32))

    def bank(i, n=512, p=128):
        return PSUM[0:p, i * 512:i * 512 + n]

    sb = kb.sbuf
    c_ident = sb("c_ident", [128, 128], BF16)
    c_identf = sb("c_identf", [128, 128], F32)
    c_revf = sb("c_revf", [128, 128], F32)
    c_ones = sb("c_ones", [128, 1], BF16)
    c_onesr = sb("c_onesr", [1, 2048], BF16)
    c_zeror = sb("c_zeror", [1, 2048], BF16)
    c_tri16 = sb("c_tri16", [16, 16], BF16)
    c_iota16 = sb("c_iota16", [16, 128], F32)
    c_eps = sb("c_eps", [1, 1], F32)

    xt = [sb("xt%d" % c, [128, 512], F32) for c in range(NCH)]
    wt = [sb("wt%d" % c, [128, 512], F32) for c in range(4)]
    pA = None
    hiB = [sb("hiB_%d" % c, [128, 512], BF16) for c in range(NCH)]
    loB = [sb("loB_%d" % c, [128, 512], BF16) for c in range(NCH)]
    rowA = sb("rowA", [1, 2048], F32)
    rowB = sb("rowB", [1, 512], F32)
    rowC = sb("rowC", [1, 512], F32)
    rowD = sb("rowD", [1, 512], F32)
    bcA = sb("bcA", [128, 512], F32)
    bcB = sb("bcB", [128, 512], F32)

    pA = ExitStack()
    hi8 = [sb("hi8_%d" % c, [128, 512], BF16, pA) for c in range(NCH)]
    lo8 = [sb("lo8_%d" % c, [128, 512], BF16, pA) for c in range(NCH)]
    tkt = [sb("tkt%d" % i, [128, 128], BF16, pA) for i in range(4)]

    gp.load_library(LC.mlp)

    DQ = "dq"

    def ld(out, in_, q=sy, tag=DQ, **kw):
        kb.mark(q.dma_start(out=out, in_=in_, **kw), tag, 16)

    st = ld

    def wdma(eng, tag=DQ):
        kb.need(eng, tag)

    ld(c_ident[:], dident[:, :]); ld(c_identf[:], didentf[:, :]); ld(c_revf[:], drevf[:, :])
    ld(c_ones[:], donesc[:, :])
    ld(c_onesr[:], donesr[:, :]); ld(c_zeror[:], dzeror[:, :])
    ld(c_tri16[:], dtri16[:, :])
    ld(c_iota16[:], diota16[:, :])
    kb.mark(v.memset(c_eps[:, :], LN_EPS), "ceps")

    def split(src32, hi, lo, chain="vch"):
        kb.mark(v.tensor_copy(out=hi, in_=src32), chain)
        kb.need(v, chain)
        kb.mark(v.tensor_tensor(out=lo, in0=src32, in1=hi, op=ALU.subtract), chain)
        kb.need(v, chain)

    def layer_norm_tiles(src_tiles, dst_tiles, ch, gam=None, bet=None):
        for e_ in (sc, v, t, gp):
            kb.need(e_, "ev"); kb.need(e_, "act"); kb.need(e_, "mm"); kb.need(e_, "st")
        for c in range(NCH):
            kb.need(sc, ch + "w")
            kb.mark(sc.activation(out=wt[0][:], in_=src_tiles[c][:], func=AF.Square), ch + "sq")
            split(src_tiles[c][:], hiB[c][:], loB[c][:], ch + "v")
            kb.need(v, ch + "sq")
            split(wt[0][:], hi8[c][:], lo8[c][:], ch + "v")
            kb.mark(v.nop(), ch + "w")
            kb.need(t, ch + "v")
            t.matmul(bank(6, 512, 1), c_ones[:, :], hiB[c][:], start=(c == 0), stop=False)
            t.matmul(bank(6, 512, 1), c_ones[:, :], loB[c][:], start=False, stop=(c == NCH - 1))
            t.matmul(bank(7, 512, 1), c_ones[:, :], hi8[c][:], start=(c == 0), stop=False)
            kb.mark(t.matmul(bank(7, 512, 1), c_ones[:, :], lo8[c][:], start=False,
                             stop=(c == NCH - 1)), ch + "mm")
        kb.need(v, ch + "mm")
        kb.need(v, "lng")
        mu = rowA[0:1, 0:512]; msq = rowB[0:1, 0:512]
        var = rowC[0:1, 0:512]; rstd = rowD[0:1, 0:512]
        kb.mark(v.tensor_scalar(out=mu, in0=bank(6, 512, 1), scalar1=1.0 / D, scalar2=None, op0=ALU.mult), ch + "r")
        kb.mark(v.tensor_scalar(out=msq, in0=bank(7, 512, 1), scalar1=1.0 / D, scalar2=None, op0=ALU.mult), ch + "r")
        kb.need(v, ch + "r")
        kb.mark(v.nop(), ch + "mmf")
        kb.mark(v.tensor_tensor(out=var, in0=mu, in1=mu, op=ALU.mult), ch + "r")
        kb.need(v, ch + "r")
        kb.mark(v.tensor_tensor(out=var, in0=msq, in1=var, op=ALU.subtract), ch + "r")
        kb.need(sc, ch + "r")
        kb.need(sc, "ceps")
        kb.mark(sc.activation(out=var, in_=var, func=AF.Sqrt, bias=c_eps[0:1, 0:1]), ch + "s")
        kb.need(v, ch + "s")
        kb.mark(v.reciprocal(out=rstd, in_=var), ch + "r2")
        kb.need(gp, ch + "r2")
        kb.mark(gp.partition_broadcast(bcA[:], mu), ch + "g")
        kb.mark(gp.partition_broadcast(bcB[:], rstd), ch + "g")
        kb.mark(gp.nop(), "lng", 2)
        kb.need(v, ch + "g")
        for c in range(NCH):
            kb.mark(v.tensor_tensor(out=wt[1][:], in0=src_tiles[c][:], in1=bcA[:], op=ALU.subtract), ch + "a")
            kb.need(v, ch + "a")
            if gam is None:
                kb.mark(v.tensor_tensor(out=dst_tiles[c][:], in0=wt[1][:], in1=bcB[:], op=ALU.mult), ch + "a")
            else:
                kb.mark(v.tensor_tensor(out=wt[1][:], in0=wt[1][:], in1=bcB[:], op=ALU.mult), ch + "a")
                kb.need(v, ch + "a")
                kb.mark(v.tensor_scalar(out=dst_tiles[c][:], in0=wt[1][:], scalar1=gam(c),
                                        scalar2=bet(c), op0=ALU.mult, op1=ALU.add), ch + "a")
            kb.need(v, ch + "a")

    if PHASE_LIMIT >= 1:
        p1 = ExitStack()
        c_rsin = sb("c_rsin", [128, T], F32, p1)
        c_rcos = sb("c_rcos", [128, T], F32, p1)
        ld(c_rsin[:], drsin[:, :]); ld(c_rcos[:], drcos[:, :])
        for tt in range(NT512):
            L = tt * 512
            kb.tile_sync()
            for c in range(NCH):
                ld(xt[c][:], x_in[c * 128:(c + 1) * 128, L:L + 512])
            for e_ in (sc, v, t, gp):
                wdma(e_, DQ)
            ch0 = "ln"
            layer_norm_tiles(xt, xt, ch0)
            for c in range(NCH):
                split(xt[c][:], hiB[c][:], loB[c][:], "ev")
            kb.need(sc, "lna"); kb.need(sc, "ev")
            for c in range(NCH):
                st(xn32d[c * 128:(c + 1) * 128, L:L + 512], xt[c][:], q=sc, tag="st")
                st(xnH[c * 128:(c + 1) * 128, L:L + 512], hiB[c][:], q=sc, tag="st")
                st(xnL[c * 128:(c + 1) * 128, L:L + 512], loB[c][:], q=sc, tag="st")
            tcol = (tt % 4) * 512
            cosm = c_rcos[:, tcol:tcol + 512]
            sinm = c_rsin[:, tcol:tcol + 512]
            for p in range(4):
                e_, o_ = xt[p][:], xt[p + 4][:]
                kb.mark(v.tensor_tensor(out=wt[0][:], in0=e_, in1=cosm, op=ALU.mult), "ev")
                kb.mark(v.tensor_tensor(out=wt[1][:], in0=o_, in1=sinm, op=ALU.mult), "ev")
                kb.mark(v.tensor_tensor(out=wt[2][:], in0=e_, in1=sinm, op=ALU.mult), "ev")
                kb.mark(v.tensor_tensor(out=wt[3][:], in0=o_, in1=cosm, op=ALU.mult), "ev")
                kb.need(v, "ev")
                kb.mark(v.tensor_tensor(out=wt[0][:], in0=wt[0][:], in1=wt[1][:], op=ALU.subtract), "ev")
                kb.mark(v.tensor_tensor(out=wt[2][:], in0=wt[2][:], in1=wt[3][:], op=ALU.add), "ev")
                kb.need(v, "ev")
                split(wt[0][:], hi8[p][:], lo8[p][:], "ev")
                split(wt[2][:], hi8[p + 4][:], lo8[p + 4][:], "ev")
                kb.need(sc, "ev")
                for pp in (p, p + 4):
                    st(rHd[pp * 128:(pp + 1) * 128, L:L + 512], hi8[pp][:], q=sc, tag="st")
                    st(rLd[pp * 128:(pp + 1) * 128, L:L + 512], lo8[pp][:], q=sc, tag="st")
            kb.mark(v.nop(), "p1tile")
        kb.barrier()
        p1.close()

    if PHASE_LIMIT >= 2:
        p2 = ExitStack()
        uTh = sb("uTh", [64, 2 * NTOK], BF16, p2)
        uTl = sb("uTl", [64, 2 * NTOK], BF16, p2)
        u32 = sb("u32", [64, 512], F32, p2)
        Rh = sb("Rh", [64, 129 * 64], BF16, p2)
        Rl = sb("Rl", [64, 129 * 64], BF16, p2)
        Ah = sb("Ah", [64, 64], BF16, p2)
        Al = sb("Al", [64, 64], BF16, p2)
        r32 = sb("r32", [64, 64], F32, p2)
        LT = sb("LT", [64, NTOK], F32, p2)
        cT = sb("cT", [64, 34], F32, p2)
        cTh = sb("cTh", [64, 34], BF16, p2)
        cTl = sb("cTl", [64, 34], BF16, p2)
        sgb = sb("sgb", [128, 16], F32, p2)
        bmH = sb("bmH", [128, 512], BF16, p2)
        bmL = sb("bmL", [128, 512], BF16, p2)
        ctH = sb("ctH", [64, 1024], BF16, p2)
        ctL = sb("ctL", [64, 1024], BF16, p2)

        ld(Ah[:], dA[0][:, :]); ld(Al[:], dA[1][:, :])
        ld(bmH[:, 0:512].rearrange("p (c s) -> p c s", c=8), dBmT[0][:, :].rearrange("(c p) s -> p c s", p=128))
        ld(bmL[:, 0:512].rearrange("p (c s) -> p c s", c=8), dBmT[1][:, :].rearrange("(c p) s -> p c s", p=128))
        ld(sgb[:, 0:16].rearrange("p (c g) -> p c g", c=8), d_sgb[:, :].rearrange("(c p) g -> p c g", p=128))
        for tt in range(NT512):
            L = tt * 512
            kb.tile_sync((sy,))
            for c in range(NCH):
                ld(hiB[c][:], xnH[c * 128:(c + 1) * 128, L:L + 512], tag="x")
                ld(loB[c][:], xnL[c * 128:(c + 1) * 128, L:L + 512], tag="x")
            wdma(t, DQ); wdma(t, "x")
            kb.need(t, "ev")
            for c in range(NCH):
                kb.mm3(bank(0, 512, 64), bmH[:, c * 64:(c + 1) * 64], bmL[:, c * 64:(c + 1) * 64],
                       hiB[c][:], loB[c][:], start=(c == 0), stop=(c == NCH - 1),
                       mark=("mm" if c == NCH - 1 else None))
            kb.need(v, "mm")
            kb.mark(v.tensor_copy(out=u32[:, :], in_=bank(0, 512, 64)), "ev")
            kb.need(v, "ev")
            for j_ in range(4):
                g_ = 4 * tt + j_
                split(u32[:, j_ * 128:(j_ + 1) * 128],
                      uTh[:, g_ * 256 + 128:g_ * 256 + 256],
                      uTl[:, g_ * 256 + 128:g_ * 256 + 256], "ev")
            if tt == 0:
                for g_ in range(NTOK // 128):
                    kb.mark(v.memset(uTh[:, g_ * 256:g_ * 256 + 128], 0.0), "ev")
                    kb.mark(v.memset(uTl[:, g_ * 256:g_ * 256 + 128], 0.0), "ev")
                kb.need(v, "ev")
        kb.mark(v.tensor_copy(out=Rh[:, 0:64], in_=c_ident[0:64, 0:64]), "ev")
        kb.need(v, "ev")
        kb.mark(v.memset(Rl[:, 0:64], 0.0), "ev")
        kb.need(v, "ev")
        wdma(t, DQ)
        for dd in range(128):
            kb.need(t, "ev")
            kb.mm3(bank(1, 64, 64), Ah[:, :], Al[:, :],
                   Rh[:, dd * 64:(dd + 1) * 64], Rl[:, dd * 64:(dd + 1) * 64], True, True, mark="mm")
            kb.need(v, "mm")
            kb.mark(v.tensor_copy(out=r32[:, :], in_=bank(1, 64, 64)), "ev")
            kb.need(v, "ev")
            split(r32[:, :], Rh[:, (dd + 1) * 64:(dd + 2) * 64], Rl[:, (dd + 1) * 64:(dd + 2) * 64], "ev")
        kb.need(t, "ev"); kb.need(t, "ev")
        for dd in range(128):
            lh = Rh[:, dd * 64:(dd + 1) * 64]
            ll = Rl[:, dd * 64:(dd + 1) * 64]
            for bk in range(8):
                off = bk * 1024 + 128 - dd
                rh = bass.AP(uTh[:, :].tensor, off, [[2 * NTOK, 64], [256, 4], [1, 128]])
                rl = bass.AP(uTl[:, :].tensor, off, [[2 * NTOK, 64], [256, 4], [1, 128]])
                ob = bank(bk, 512, 64)
                t.matmul(ob, lh, rh, start=(dd == 0), stop=False)
                t.matmul(ob, lh, rl, start=False, stop=False)
                _last_p1 = t.matmul(ob, ll, rh, start=False, stop=(dd == 127))
        kb.mark(_last_p1, "mm")
        kb.need(v, "mm")
        kb.mark(v.tensor_copy(out=LT[:, :], in_=PSUM[0:64, :]), "ev")
        kb.need(v, "ev")
        kb.mark(v.nop(), "ev")
        kb.mark(v.memset(cT[:, :], 0.0), "ev")
        kb.need(v, "ev")
        kb.mark(v.tensor_copy(out=cTh[:, 0:1], in_=cT[:, 0:1]), "ev")
        kb.mark(v.tensor_copy(out=cTh[:, 17:18], in_=cT[:, 17:18]), "ev")
        kb.mark(v.tensor_copy(out=cTl[:, 0:1], in_=cT[:, 0:1]), "ev")
        kb.mark(v.tensor_copy(out=cTl[:, 17:18], in_=cT[:, 17:18]), "ev")
        kb.need(v, "ev")
        R128h = Rh[:, 128 * 64:129 * 64]
        R128l = Rl[:, 128 * 64:129 * 64]
        cview = lambda tens, C: bass.AP(tens[:, :].tensor, C, [[34, 64], [17, 2]])
        kb.need(t, "ev")
        for Ci in range(16):
            kb.need(t, "ev")
            kb.mm3(bank(1, 2, 64), R128h, R128l, cview(cTh, Ci), cview(cTl, Ci), True, True, mark="mm")
            kb.need(v, "mm")
            lcol = bass.AP(LT[:, :].tensor, 128 * Ci + 127, [[NTOK, 64], [T, 2]])
            kb.mark(v.tensor_tensor(out=cview(cT, Ci + 1), in0=bank(1, 2, 64), in1=lcol, op=ALU.add), "ev")
            kb.need(v, "ev")
            kb.mark(v.tensor_copy(out=cview(cTh, Ci + 1), in_=cview(cT, Ci + 1)), "ev")
            kb.need(v, "ev")
            kb.mark(v.tensor_tensor(out=cview(cT, Ci + 1), in0=cview(cT, Ci + 1), in1=cview(cTh, Ci + 1),
                                    op=ALU.subtract), "ev")
            kb.need(v, "ev")
            kb.mark(v.tensor_copy(out=cview(cTl, Ci + 1), in_=cview(cT, Ci + 1)), "ev")
            kb.need(v, "ev")
        crhs_h = bass.AP(cTh[:, :].tensor, 0, [[34, 64], [17, 2], [1, 16]])
        crhs_l = bass.AP(cTl[:, :].tensor, 0, [[34, 64], [17, 2], [1, 16]])
        kb.need(t, "ev")
        for i in range(128):
            bk = 1 + (i % 2)
            kb.need(t, "p2s%d" % (i % 2))
            kb.mm3(bank(bk, 32, 64), Rh[:, (i + 1) * 64:(i + 2) * 64], Rl[:, (i + 1) * 64:(i + 2) * 64],
                   crhs_h, crhs_l, True, True, mark="mm")
            kb.need(v, "mm")
            dst = bass.AP(LT[:, :].tensor, i, [[NTOK, 64], [T, 2], [128, 16]])
            kb.mark(v.tensor_tensor(out=dst, in0=bank(bk, 32, 64), in1=dst, op=ALU.add), "p2s%d" % (i % 2))
            kb.need(v, "p2s%d" % (i % 2))
        LTh = uTh[:, 0:NTOK]
        LTl = uTh[:, NTOK:2 * NTOK]
        split(LT[:, :], LTh, LTl, "ev")
        ld(ctH[:, :], dCT[0][:, :], tag="w")
        ld(ctL[:, :], dCT[1][:, :], tag="w")
        wdma(t, "w")
        kb.need(t, "p2s0"); kb.need(t, "p2s1"); kb.need(t, "ev")
        for tt in range(NT512):
            L = tt * 512
            kb.tile_sync()
            for oc in range(8):
                bk = oc % 2
                kb.need(t, "p2g%d" % bk)
                kb.mm3(bank(bk), ctH[:, oc * 128:(oc + 1) * 128], ctL[:, oc * 128:(oc + 1) * 128],
                       LTh[:, L:L + 512], LTl[:, L:L + 512], True, True, mark="mm")
                kb.need(sc, "mm")
                kb.mark(sc.activation(out=xt[oc][:], in_=bank(bk), func=AF.Gelu), "p2g%d" % bk)
                kb.need(sc, "p2g%d" % bk)
            for e_ in (v, t, gp):
                kb.need(e_, "p2g0"); kb.need(e_, "p2g1")
            layer_norm_tiles(xt, xt, "ln",
                             gam=lambda c: sgb[:, c * 2:c * 2 + 1],
                             bet=lambda c: sgb[:, c * 2 + 1:c * 2 + 2])
            for c in range(NCH):
                kb.mark(v.tensor_scalar(out=xt[c][:], in0=xt[c][:], scalar1=CLAMP, scalar2=-CLAMP,
                                        op0=ALU.min, op1=ALU.max), "ev")
                kb.need(v, "ev")
            kb.need(sc, "ev")
            for c in range(NCH):
                st(s1d[c * 128:(c + 1) * 128, L:L + 512], xt[c][:], q=sc, tag="st")
        kb.barrier()
        p2.close()

    if PHASE_LIMIT >= 3:
        p3 = ExitStack()
        wsH = sb("wsH", [128, 8192], BF16, p3)
        wsL = sb("wsL", [128, 8192], BF16, p3)
        qkvb = sb("qkvb_s", [128, 24], F32, p3)
        ld(qkvb[:, 0:24].rearrange("p (c g) -> p c g", c=8),
           dqkvb[:, :].rearrange("(c p) g -> p c g", p=128))
        for ip, (dW, srcH, srcL, dstH, dstL, bcol, scl) in enumerate((
                (dWq, rHd, rLd, qHd, qLd, 0, 0.125),
                (dWk, rHd, rLd, kHd, kLd, 1, None),
                (dWv, xnH, xnL, vHd, vLd, 2, None))):
            tg = "p3w%d" % ip
            kb.need(sy, "mm"); kb.need(sy, "ev")
            for kc in range(8):
                ld(wsH[:, kc * 1024:(kc + 1) * 1024], dW[0][kc * 128:(kc + 1) * 128, :], tag=tg)
                ld(wsL[:, kc * 1024:(kc + 1) * 1024], dW[1][kc * 128:(kc + 1) * 128, :], tag=tg)
            wdma(t, tg); wdma(v, DQ)
            for tt in range(NT512):
                L = tt * 512
                kb.tile_sync()
                for c in range(NCH):
                    ld(hiB[c][:], srcH[c * 128:(c + 1) * 128, L:L + 512], tag="x")
                    ld(loB[c][:], srcL[c * 128:(c + 1) * 128, L:L + 512], tag="x")
                wdma(t, "x")
                for oc in range(8):
                    bk = oc % 4
                    kb.need(t, "ev")
                    for kc in range(8):
                        kb.mm3(bank(bk), wsH[:, kc * 1024 + oc * 128:kc * 1024 + (oc + 1) * 128],
                               wsL[:, kc * 1024 + oc * 128:kc * 1024 + (oc + 1) * 128],
                               hiB[kc][:], loB[kc][:], start=(kc == 0), stop=(kc == 7),
                               mark=("mm" if kc == 7 else None))
                    kb.need(v, "mm"); kb.need(v, "st")
                    if scl is not None:
                        kb.mark(v.tensor_scalar(out=wt[3][:], in0=bank(bk),
                                                scalar1=qkvb[:, oc * 3 + bcol:oc * 3 + bcol + 1],
                                                scalar2=scl, op0=ALU.add, op1=ALU.mult), "ev")
                    else:
                        kb.mark(v.tensor_scalar(out=wt[3][:], in0=bank(bk),
                                                scalar1=qkvb[:, oc * 3 + bcol:oc * 3 + bcol + 1],
                                                scalar2=None, op0=ALU.add), "ev")
                    kb.need(v, "ev")
                    kb.mark(v.nop(), "ev")
                    split(wt[3][:], hi8[oc][:], lo8[oc][:], "ev")
                    kb.need(sc, "ev")
                    st(dstH[oc * 128:(oc + 1) * 128, L:L + 512], hi8[oc][:], q=sc, tag="st")
                    st(dstL[oc * 128:(oc + 1) * 128, L:L + 512], lo8[oc][:], q=sc, tag="st")
        kb.barrier()
        p3.close()

    if PHASE_LIMIT >= 4:
        p4 = ExitStack()
        qsh = sb("qsh", [65, 2048], BF16, p4)
        qsl = sb("qsl", [65, 2048], BF16, p4)
        ksh = sb("ksh", [65, 2048], BF16, p4)
        ksl = sb("ksl", [65, 2048], BF16, p4)
        vth = sb("vth", [128, 1024], BF16, p4)
        vtl = sb("vtl", [128, 1024], BF16, p4)
        et32 = sb("et32", [128, 512], F32, p4)
        eh = sb("eh", [128, 512], BF16, p4)
        el = sb("el", [128, 512], BF16, p4)
        zin = sb("zin", [64, 512], F32, p4)
        cx32 = sb("cx32", [64, 512], F32, p4)
        cxh = sb("cxh", [64, 512], BF16, p4)
        cxl = sb("cxl", [64, 512], BF16, p4)
        qn2 = sb("qn2", [64, 2048], F32, p4)
        qn2b = sb("qn2b", [64, 2048], BF16, p4)
        c_mask4 = sb("c_mask4", [128, 4 * 512], BF16, p4)
        krow = sb("krow", [1, 2048], F32, p4)
        brH = sb("brH", [1, 2048], BF16, p4)
        brL = sb("brL", [1, 2048], BF16, p4)
        ld(c_mask4[:], dmask4[:, :])
        for b in range(BPC):
            for h in range(H):
                r0, c0 = 64 * h, b * T
                kb.tile_sync((sy, sc))
                ld(qsh[0:64, :], qHd[r0:r0 + 64, c0:c0 + T], tag="x")
                ld(qsl[0:64, :], qLd[r0:r0 + 64, c0:c0 + T], tag="x")
                ld(ksh[0:64, :], kHd[r0:r0 + 64, c0:c0 + T], tag="x")
                ld(ksl[0:64, :], kLd[r0:r0 + 64, c0:c0 + T], tag="x")
                ld(ksh[64:65, :], c_onesr[0:1, 0:2048], tag="x")
                ld(ksl[64:65, :], c_zeror[0:1, 0:2048], tag="x")
                for kc in range(16):
                    kb.mark(sc.dma_start_transpose(
                        out=vth[:, kc * 64:(kc + 1) * 64],
                        in_=vHd[r0:r0 + 64, c0 + kc * 128:c0 + (kc + 1) * 128]), "vt", 16)
                    kb.mark(sc.dma_start_transpose(
                        out=vtl[:, kc * 64:(kc + 1) * 64],
                        in_=vLd[r0:r0 + 64, c0 + kc * 128:c0 + (kc + 1) * 128]), "vt", 16)
                wdma(sc, "x")
                kb.need(sc, "act")
                kb.mark(sc.activation(out=qn2[:, :], in_=qsh[0:64, :], func=AF.Square), "act")
                kb.need(v, "act"); kb.need(t, "ev")
                kb.mark(v.tensor_copy(out=qn2b[:, :], in_=qn2[:, :]), "ev")
                kb.need(t, "ev")
                for j in range(4):
                    kb.mark(t.matmul(PSUM[0:1, (4 + j) * 512:(5 + j) * 512], c_ones[0:64, :],
                                     qn2b[:, j * 512:(j + 1) * 512], start=True, stop=True), "mm")
                kb.need(sc, "mm")
                kb.mark(sc.activation(out=rowA[0:1, :], in_=PSUM[0:1, 2048:4096], func=AF.Sqrt), "act")
                kb.need(sc, "ev")
                kb.mark(sc.activation(out=qn2[:, :], in_=ksh[0:64, :], func=AF.Square), "act")
                kb.need(v, "act")
                kb.mark(v.tensor_copy(out=qn2b[:, :], in_=qn2[:, :]), "ev")
                kb.need(t, "ev"); kb.need(t, "act")
                for j in range(4):
                    kb.mark(t.matmul(PSUM[0:1, (4 + j) * 512:(5 + j) * 512], c_ones[0:64, :],
                                     qn2b[:, j * 512:(j + 1) * 512], start=True, stop=True), "mm")
                kb.need(sc, "mm")
                kb.mark(sc.activation(out=krow[0:1, :], in_=PSUM[0:1, 2048:4096], func=AF.Sqrt), "act")
                kb.need(v, "act"); kb.need(v, "act")
                kb.mark(v.tensor_reduce(out=krow[0:1, 0:1], in_=krow[0:1, :], axis=AXL.X, op=ALU.max), "ev")
                kb.need(v, "ev")
                kb.mark(v.tensor_scalar(out=rowA[0:1, :], in0=rowA[0:1, :], scalar1=krow[0:1, 0:1],
                                        scalar2=-0.1262626, op0=ALU.mult, op1=ALU.mult), "ev")
                kb.need(v, "ev"); kb.need(v, "mm")
                split(rowA[0:1, :], brH[0:1, :], brL[0:1, :], "ev")
                kb.need(sy, "ev")
                ld(qsh[64:65, :], brH[0:1, :], tag="x")
                ld(qsl[64:65, :], brL[0:1, :], tag="x")
                wdma(t, "x"); kb.need(t, "ev"); wdma(t, "vt")
                for qc in range(4):
                    qs = qc * 512
                    kcl = list(range(0, 4 * qc + 4))
                    nk = len(kcl)
                    for ik, kc in enumerate(kcl):
                        kb.need(t, "ev")
                        kb.mm3(bank(0), ksh[:, kc * 128:(kc + 1) * 128], ksl[:, kc * 128:(kc + 1) * 128],
                               qsh[:, qs:qs + 512], qsl[:, qs:qs + 512], True, True, mark="mm")
                        kb.need(sc, "mm"); kb.need(sc, "ev")
                        kb.mark(sc.activation(out=et32[:, :], in_=bank(0), func=AF.Exp), "act")
                        kb.need(v, "act"); kb.need(v, "mm")
                        j = kc - 4 * qc
                        if j >= 0:
                            kb.mark(v.tensor_tensor(out=et32[:, :], in0=et32[:, :],
                                                    in1=c_mask4[:, j * 512:(j + 1) * 512], op=ALU.mult), "ev")
                            kb.need(v, "ev")
                        split(et32[:, :], eh[:, :], el[:, :], "ev")
                        kb.mark(v.nop(), "ev")
                        kb.need(t, "ev")
                        t.matmul(PSUM[0:1, 1024:1536], c_ones[:, :], eh[:, :],
                                 start=(ik == 0), stop=False)
                        kb.mark(t.matmul(PSUM[0:1, 1024:1536], c_ones[:, :], el[:, :],
                                         start=False, stop=(ik == nk - 1)), "mm")
                        kb.mm3(bank(3, 512, 64), vth[:, kc * 64:(kc + 1) * 64], vtl[:, kc * 64:(kc + 1) * 64],
                               eh[:, :], el[:, :], start=(ik == 0), stop=(ik == nk - 1), mark="mm")
                    kb.need(v, "mm"); kb.need(v, "gb")
                    kb.mark(v.tensor_copy(out=rowD[0:1, 0:512], in_=PSUM[0:1, 1024:1536]), "ev")
                    kb.need(v, "ev")
                    kb.mark(v.reciprocal(out=rowD[0:1, 0:512], in_=rowD[0:1, 0:512]), "ev")
                    kb.need(gp, "ev")
                    kb.mark(gp.partition_broadcast(zin[:, :], rowD[0:1, 0:512]), "gb")
                    kb.need(v, "gb"); kb.need(v, "mm")
                    kb.mark(v.tensor_tensor(out=cx32[:, :], in0=bank(3, 512, 64), in1=zin[:, :],
                                            op=ALU.mult), "ev")
                    kb.need(v, "ev"); kb.need(v, "st")
                    split(cx32[:, :], cxh[:, :], cxl[:, :], "ev")
                    kb.need(sc, "ev")
                    st(cxHd[r0:r0 + 64, c0 + qs:c0 + qs + 512], cxh[:, :], q=sc, tag="st")
                    st(cxLd[r0:r0 + 64, c0 + qs:c0 + qs + 512], cxl[:, :], q=sc, tag="st")
                    kb.need(t, "ev")
        kb.barrier()
        p4.close()

    if PHASE_LIMIT >= 5:
        p5 = ExitStack()
        wsH = sb("wsH5", [128, 8192], BF16, p5)
        wsL = sb("wsL5", [128, 8192], BF16, p5)
        lnb0 = sb("lnb0_s", [128, 40], F32, p5)
        ld(lnb0[:, 0:40].rearrange("p (c g) -> p c g", c=8),
           dlnb0[:, :].rearrange("(c p) g -> p c g", p=128))
        for kc in range(8):
            ld(wsH[:, kc * 1024:(kc + 1) * 1024], dWo[0][kc * 128:(kc + 1) * 128, :], tag="w")
            ld(wsL[:, kc * 1024:(kc + 1) * 1024], dWo[1][kc * 128:(kc + 1) * 128, :], tag="w")
        wdma(t, "w"); wdma(v, DQ)
        for tt in range(NT512):
            L = tt * 512
            kb.tile_sync()
            for c in range(NCH):
                ld(hiB[c][:], cxHd[c * 128:(c + 1) * 128, L:L + 512], tag="x")
                ld(loB[c][:], cxLd[c * 128:(c + 1) * 128, L:L + 512], tag="x")
                ld(xt[c][:], xn32d[c * 128:(c + 1) * 128, L:L + 512], tag="x")
            wdma(t, "x"); wdma(v, "x"); wdma(sc, "x"); wdma(gp, "x")
            for oc in range(8):
                bk = oc % 4
                kb.need(t, "ev")
                for kc in range(8):
                    kb.mm3(bank(bk), wsH[:, kc * 1024 + oc * 128:kc * 1024 + (oc + 1) * 128],
                           wsL[:, kc * 1024 + oc * 128:kc * 1024 + (oc + 1) * 128],
                           hiB[kc][:], loB[kc][:], start=(kc == 0), stop=(kc == 7),
                           mark=("mm" if kc == 7 else None))
                kb.need(v, "mm")
                kb.mark(v.tensor_scalar(out=wt[3][:], in0=bank(bk), scalar1=lnb0[:, oc * 5:oc * 5 + 1],
                                        scalar2=None, op0=ALU.add), "ev")
                kb.need(v, "ev")
                kb.mark(v.nop(), "ev")
                kb.mark(v.tensor_tensor(out=xt[oc][:], in0=wt[3][:], in1=xt[oc][:], op=ALU.add), "ev")
                kb.need(v, "ev")
            ch0 = "ln"
            layer_norm_tiles(xt, xt, "ln",
                             gam=lambda c: lnb0[:, c * 5 + 1:c * 5 + 2],
                             bet=lambda c: lnb0[:, c * 5 + 2:c * 5 + 3])
            for c in range(NCH):
                split(xt[c][:], hiB[c][:], loB[c][:], "ev")
            kb.need(sc, "lna"); kb.need(sc, "ev")
            for c in range(NCH):
                st(x1d[c * 128:(c + 1) * 128, L:L + 512], xt[c][:], q=sc, tag="st")
                st(x1H[c * 128:(c + 1) * 128, L:L + 512], hiB[c][:], q=sc, tag="st")
                st(x1L[c * 128:(c + 1) * 128, L:L + 512], loB[c][:], q=sc, tag="st")
        kb.barrier()
        p5.close()

    if PHASE_LIMIT >= 6:
        p6 = ExitStack()
        wsH = sb("wsH6", [128, 8192], BF16, p6)
        wsL = sb("wsL6", [128, 8192], BF16, p6)
        lnb0 = sb("lnb06", [128, 40], F32, p6)
        gwh = sb("gwh", [128, 64], BF16, p6)
        gwl = sb("gwl", [128, 64], BF16, p6)
        gb0b = sb("gb0b", [128, 8], F32, p6)
        ebs = sb("ebs", [128, 64], F32, p6)
        glt = sb("glt", [128, 8], F32, p6)
        gs8 = sb("gs8", [128, 8], F32, p6)
        gth = sb("gth", [128, 8], F32, p6)
        gw1t = sb("gw1t", [128, 24], F32, p6)
        zero8 = sb("zero8", [128, 8], F32, p6)
        mt8 = sb("mt8", [8, 128], F32, p6)
        mrow = sb("mrow", [1, 512], F32, p6)
        gtb = sb("gtb", [128, 512], F32, p6)
        mtTd = kb.dram_scratch("mtT", [8, NTOK], F32)
        accT = [sb("p6at%d" % c, [128, 2048], F32, p6) for c in range(NCH)]

        ld(lnb0[:, 0:40].rearrange("p (c g) -> p c g", c=8),
           dlnb0[:, :].rearrange("(c p) g -> p c g", p=128))
        ld(gwh[:, :].rearrange("p (c e) -> p c e", c=8), dgw0[0][:, :].rearrange("(c p) e -> p c e", p=128))
        ld(gwl[:, :].rearrange("p (c e) -> p c e", c=8), dgw0[1][:, :].rearrange("(c p) e -> p c e", p=128))
        ld(ebs[:, :].rearrange("p (c e) -> p c e", c=8), dEB[:, :].rearrange("(c p) e -> p c e", p=128))
        ld(rowA[0:1, 0:8], dgb0[:, :])
        wdma(gp, DQ); wdma(v, DQ)
        kb.mark(gp.partition_broadcast(gb0b[:, :], rowA[0:1, 0:8]), "gb")
        kb.mark(v.memset(zero8[:, :], 0.0), "ev")
        for tt in range(NT512):
            L = tt * 512
            kb.tile_sync()
            for c in range(NCH):
                ld(hiB[c][:], x1H[c * 128:(c + 1) * 128, L:L + 512], tag="x")
                ld(loB[c][:], x1L[c * 128:(c + 1) * 128, L:L + 512], tag="x")
            wdma(t, "x")
            for kc in range(8):
                kb.mm3(PSUM[0:8, 0:512], gwh[:, kc * 8:(kc + 1) * 8], gwl[:, kc * 8:(kc + 1) * 8],
                       hiB[kc][:], loB[kc][:], start=(kc == 0), stop=(kc == 7),
                       mark=("mm" if kc == 7 else None))
            kb.need(v, "mm")
            kb.mark(v.tensor_copy(out=wt[0][0:8, :], in_=PSUM[0:8, 0:512]), "ev")
            kb.need(t, "ev")
            for j in range(4):
                jt = tt * 4 + j
                kb.need(t, "ev")
                kb.mark(t.transpose(PSUM[0:128, 512:520], wt[0][0:8, j * 128:(j + 1) * 128],
                                    c_identf[0:8, 0:8]), "mm")
                kb.need(v, "mm"); kb.need(v, "gb")
                kb.mark(v.tensor_tensor(out=glt[:, :], in0=PSUM[0:128, 512:520], in1=gb0b[:, :],
                                        op=ALU.add), "ev")
                kb.need(v, "ev")
                kb.mark(v.nop(), "evt")
                kb.mark(v.tensor_reduce(out=gw1t[:, 0:1], in_=glt[:, :], axis=AXL.X, op=ALU.max), "ev")
                kb.need(v, "ev")
                kb.mark(v.tensor_scalar(out=glt[:, :], in0=glt[:, :], scalar1=gw1t[:, 0:1],
                                        scalar2=None, op0=ALU.subtract), "ev")
                kb.need(sc, "ev")
                kb.mark(sc.activation(out=glt[:, :], in_=glt[:, :], func=AF.Exp,
                                      accum_out=gw1t[:, 1:2]), "act")
                kb.need(v, "act")
                kb.mark(v.reciprocal(out=gw1t[:, 2:3], in_=gw1t[:, 1:2]), "ev")
                kb.need(v, "ev")
                kb.mark(v.tensor_scalar(out=glt[:, :], in0=glt[:, :], scalar1=gw1t[:, 2:3],
                                        scalar2=None, op0=ALU.mult), "ev")
                kb.need(v, "ev")
                kb.mark(v.max(out=gs8[:, :], in_=glt[:, :]), "ev")
                kb.need(v, "ev")
                kb.mark(v.tensor_scalar(out=gth[:, :], in0=glt[:, :], scalar1=gs8[:, 1:2],
                                        scalar2=None, op0=ALU.is_gt), "ev")
                kb.need(v, "ev")
                kb.mark(v.tensor_reduce(out=gw1t[:, 3:4], in_=gth[:, :], axis=AXL.X, op=ALU.add), "ev")
                kb.mark(v.tensor_scalar(out=gw1t[:, 4:12], in0=glt[:, :], scalar1=gs8[:, 1:2],
                                        scalar2=None, op0=ALU.is_equal), "ev")
                kb.need(v, "ev")
                kb.mark(v.tensor_tensor_scan(out=gw1t[:, 12:20], data0=zero8[:, :], data1=gw1t[:, 4:12],
                                             initial=0.0, op0=ALU.add, op1=ALU.add), "ev")
                kb.need(v, "ev")
                kb.mark(v.tensor_scalar(out=gw1t[:, 3:4], in0=gw1t[:, 3:4], scalar1=-1.0, scalar2=2.0,
                                        op0=ALU.mult, op1=ALU.add), "ev")
                kb.need(v, "ev")
                kb.mark(v.tensor_scalar(out=gw1t[:, 12:20], in0=gw1t[:, 12:20], scalar1=gw1t[:, 3:4],
                                        scalar2=None, op0=ALU.is_le), "ev")
                kb.need(v, "ev")
                kb.mark(v.tensor_tensor(out=gw1t[:, 4:12], in0=gw1t[:, 4:12], in1=gw1t[:, 12:20],
                                        op=ALU.mult), "ev")
                kb.need(v, "ev")
                kb.mark(v.tensor_tensor(out=gth[:, :], in0=gth[:, :], in1=gw1t[:, 4:12], op=ALU.add), "ev")
                kb.need(v, "ev")
                kb.mark(v.tensor_tensor(out=gth[:, :], in0=glt[:, :], in1=gth[:, :], op=ALU.mult), "ev")
                kb.need(t, "ev"); kb.need(t, "evt")
                kb.mark(t.transpose(PSUM[0:8, 1024:1152], gth[:, :], c_identf[0:128, 0:128]), "mm")
                kb.need(v, "mm")
                kb.mark(v.tensor_copy(out=mt8[:, :], in_=PSUM[0:8, 1024:1152]), "ev")
                kb.need(v, "ev"); kb.need(sc, "ev")
                st(mtTd[:, L + j * 128:L + (j + 1) * 128], mt8[:, :], q=sc, tag="st")
                kb.need(v, "st")
        for g in range(2):
            kb.tile_sync()
            for c in range(NCH):
                kb.mark(v.memset(accT[c][:, :], 0.0), "ev")
            kb.need(v, "ev")
            for e in range(NE):
                kb.tile_sync((sy,))
                for kc in range(8):
                    ld(wsH[:, kc * 1024:(kc + 1) * 1024], dEW[0][e, kc * 128:(kc + 1) * 128, :], tag="w")
                    ld(wsL[:, kc * 1024:(kc + 1) * 1024], dEW[1][e, kc * 128:(kc + 1) * 128, :], tag="w")
                wdma(t, "w")
                for tt4 in range(4):
                    tt = g * 4 + tt4
                    L = tt * 512
                    kb.tile_sync((sy, gp))
                    for c in range(NCH):
                        ld(hiB[c][:], x1H[c * 128:(c + 1) * 128, L:L + 512], tag="x")
                        ld(loB[c][:], x1L[c * 128:(c + 1) * 128, L:L + 512], tag="x")
                    ld(mrow[0:1, 0:512], mtTd[e:e + 1, L:L + 512], tag="x")
                    wdma(gp, "x")
                    kb.mark(gp.partition_broadcast(gtb[:, :], mrow[0:1, 0:512]), "gb")
                    wdma(t, "x")
                    for oc in range(8):
                        bk = oc % 4
                        kb.need(t, "ev")
                        for kc in range(8):
                            kb.mm3(bank(bk), wsH[:, kc * 1024 + oc * 128:kc * 1024 + (oc + 1) * 128],
                                   wsL[:, kc * 1024 + oc * 128:kc * 1024 + (oc + 1) * 128],
                                   hiB[kc][:], loB[kc][:], start=(kc == 0), stop=(kc == 7),
                                   mark=("mm" if kc == 7 else None))
                        kb.need(v, "mm"); kb.need(v, "gb")
                        kb.mark(v.scalar_tensor_tensor(out=wt[3][:], in0=bank(bk),
                                                       scalar=ebs[:, oc * 8 + e:oc * 8 + e + 1],
                                                       in1=gtb[:, :], op0=ALU.add, op1=ALU.mult), "ev")
                        kb.need(v, "ev")
                        kb.mark(v.tensor_tensor(out=accT[oc][:, tt4 * 512:(tt4 + 1) * 512],
                                                in0=accT[oc][:, tt4 * 512:(tt4 + 1) * 512],
                                                in1=wt[3][:], op=ALU.add), "ev")
                        kb.need(v, "ev")
            for tt4 in range(4):
                tt = g * 4 + tt4
                L = tt * 512
                kb.tile_sync()
                for c in range(NCH):
                    ld(xt[c][:], x1d[c * 128:(c + 1) * 128, L:L + 512], tag="x")
                wdma(v, "x")
                for c in range(NCH):
                    kb.mark(v.tensor_tensor(out=xt[c][:], in0=xt[c][:],
                                            in1=accT[c][:, tt4 * 512:(tt4 + 1) * 512], op=ALU.add), "ev")
                    kb.need(v, "ev")
                layer_norm_tiles(xt, xt, "ln",
                                 gam=lambda c: lnb0[:, c * 5 + 3:c * 5 + 4],
                                 bet=lambda c: lnb0[:, c * 5 + 4:c * 5 + 5])
                for c in range(NCH):
                    kb.need(sc, "lna")
                    kb.mark(sc.activation(out=xt[c][:], in_=xt[c][:], func=AF.Silu), "act")
                kb.need(sc, "act")
                for c in range(NCH):
                    st(g0d[c * 128:(c + 1) * 128, L:L + 512], xt[c][:], q=sc, tag="st")
        kb.barrier()
        p6.close()

    if PHASE_LIMIT >= 7:
        for tt in range(NT512):
            L = tt * 512
            kb.tile_sync()
            for c in range(NCH):
                kb.need(sy, "ev")
                ld(xt[c][:], x_in[c * 128:(c + 1) * 128, L:L + 512], tag="x")
                ld(wt[0][:], s1d[c * 128:(c + 1) * 128, L:L + 512], tag="x")
                ld(wt[1][:], g0d[c * 128:(c + 1) * 128, L:L + 512], tag="x")
                wdma(v, "x")
                kb.mark(v.tensor_tensor(out=wt[0][:], in0=wt[0][:], in1=wt[1][:], op=ALU.mult), "ev")
                kb.need(v, "ev")
                kb.mark(v.tensor_tensor(out=xt[c][:], in0=xt[c][:], in1=wt[0][:], op=ALU.add), "ev")
                kb.need(v, "ev")
                kb.need(sc, "ev")
                st(xBd[c * 128:(c + 1) * 128, L:L + 512], xt[c][:], q=sc, tag="st")
            for e_ in (sc, v, t, gp):
                wdma(e_, DQ); wdma(e_, "x")
            layer_norm_tiles(xt, xt, "ln")
            for c in range(NCH):
                split(xt[c][:], hiB[c][:], loB[c][:], "ev")
            kb.need(sc, "lna"); kb.need(sc, "ev")
            for c in range(NCH):
                st(xn2H[c * 128:(c + 1) * 128, L:L + 512], hiB[c][:], q=sc, tag="st")
                st(xn2L[c * 128:(c + 1) * 128, L:L + 512], loB[c][:], q=sc, tag="st")
        kb.barrier()
        ti = 0
        for src_, dst_ in ((xn2H, xt2H), (xn2L, xt2L)):
            for b in range(BPC):
                for ttk in range(T // 128):
                    for c in range(NCH):
                        sl = ti % 4; ti += 1
                        buf = tkt[sl]
                        kb.need(sc, "tk%d" % sl)
                        kb.mark(sc.dma_start_transpose(
                            out=buf[:, :],
                            in_=src_[c * 128:(c + 1) * 128, b * T + ttk * 128: b * T + (ttk + 1) * 128]),
                            "tv%d" % sl, 16)
                        kb.need(sy, "tv%d" % sl)
                        kb.mark(sy.dma_start(
                            out=dst_[b, ttk * 128:(ttk + 1) * 128, c * 128:(c + 1) * 128],
                            in_=buf[:, :]), "tk%d" % sl, 16)
        kb.barrier()

    if PHASE_LIMIT >= 9:
        p9 = ExitStack()
        wsH = sb("wsH9", [128, 8192], BF16, p9)
        wsL = sb("wsL9", [128, 8192], BF16, p9)
        ws2H = sb("ws2H9", [128, 8192], BF16, p9)
        ws2L = sb("ws2L9", [128, 8192], BF16, p9)
        binc = sb("binc", [128, 16], F32, p9)
        zpad = sb("zpad", [128, 768], BF16, p9)
        ld(binc[:, :], dbin[:, :])
        kb.mark(v.memset(zpad[:, :], 0.0), "ev")
        kb.need(sy, "ev")
        for ch16 in range(CCH // 128):
            for b in range(BPC):
                st(zcmH[ch16 * 128:(ch16 + 1) * 128, b * NSL:b * NSL + 256], zpad[:, 0:256], tag="st")
                st(zcmL[ch16 * 128:(ch16 + 1) * 128, b * NSL:b * NSL + 256], zpad[:, 0:256], tag="st")
                st(zcmH[ch16 * 128:(ch16 + 1) * 128, b * NSL + 2304:(b + 1) * NSL], zpad[:, :], tag="st")
                st(zcmL[ch16 * 128:(ch16 + 1) * 128, b * NSL + 2304:(b + 1) * NSL], zpad[:, :], tag="st")
        for kc in range(8):
            ld(wsH[:, kc * 1024:(kc + 1) * 1024], dWin[0][kc * 128:(kc + 1) * 128, 0:1024], tag="w")
            ld(wsL[:, kc * 1024:(kc + 1) * 1024], dWin[1][kc * 128:(kc + 1) * 128, 0:1024], tag="w")
            ld(ws2H[:, kc * 1024:(kc + 1) * 1024], dWin[0][kc * 128:(kc + 1) * 128, 1024:2048], tag="w")
            ld(ws2L[:, kc * 1024:(kc + 1) * 1024], dWin[1][kc * 128:(kc + 1) * 128, 1024:2048], tag="w")
        wdma(t, "w"); wdma(v, DQ)
        for tt in range(NT512):
            L = tt * 512
            b = tt // 4
            slot0 = 256 + (tt % 4) * 512
            kb.tile_sync()
            for c in range(NCH):
                ld(hiB[c][:], xn2H[c * 128:(c + 1) * 128, L:L + 512], tag="x")
                ld(loB[c][:], xn2L[c * 128:(c + 1) * 128, L:L + 512], tag="x")
            wdma(t, "x")
            for oc in range(16):
                bk = oc % 4
                sl = (wsH, wsL) if oc < 8 else (ws2H, ws2L)
                oc2 = oc % 8
                kb.need(t, "ev")
                for kc in range(8):
                    kb.mm3(bank(bk), sl[0][:, kc * 1024 + oc2 * 128:kc * 1024 + (oc2 + 1) * 128],
                           sl[1][:, kc * 1024 + oc2 * 128:kc * 1024 + (oc2 + 1) * 128],
                           hiB[kc][:], loB[kc][:], start=(kc == 0), stop=(kc == 7),
                           mark=("mm" if kc == 7 else None))
                kb.need(v, "mm"); kb.need(v, "st")
                kb.mark(v.tensor_scalar(out=wt[3][:], in0=bank(bk), scalar1=binc[:, oc:oc + 1],
                                        scalar2=None, op0=ALU.add), "ev")
                kb.need(v, "ev")
                split(wt[3][:], hi8[oc % 8][:], lo8[oc % 8][:], "ev")
                kb.need(sc, "ev")
                st(zcmH[oc * 128:(oc + 1) * 128, b * NSL + slot0:b * NSL + slot0 + 512],
                   hi8[oc % 8][:], q=sc, tag="st")
                st(zcmL[oc * 128:(oc + 1) * 128, b * NSL + slot0:b * NSL + slot0 + 512],
                   lo8[oc % 8][:], q=sc, tag="st")
        kb.barrier()
        p9.close()

    if PHASE_LIMIT >= 10:
        p10 = ExitStack()
        bab = sb("bab", [128, CCH], F32, p10)
        slh = [sb("slh%d" % i, [128, 384], BF16, p10) for i in range(4)]
        sll = [sb("sll%d" % i, [128, 384], BF16, p10) for i in range(4)]
        zsh = [sb("zsh%d" % i, [128, 48], BF16, p10) for i in range(4)]
        zsl = [sb("zsl%d" % i, [128, 48], BF16, p10) for i in range(4)]
        cq = sb("cq", [128, 64], F32, p10)
        sq_ = sb("sq_", [128, 64], F32, p10)
        gq = sb("gq", [128, 64], F32, p10)
        trsH = sb("trsH", [64, 128], BF16, p10)
        trsL = sb("trsL", [64, 128], BF16, p10)
        ld(rowA[0:1, 0:2048], dbar[:, :], tag="w")
        wdma(gp, "w")
        kb.mark(gp.partition_broadcast(bab[:, :], rowA[0:1, 0:2048]), "gb")
        kb.need(v, "gb")
        wexth = kb.dram["wext_hi"].ap().tensor
        wextl = kb.dram["wext_lo"].ap().tensor
        for c in range(D):
            par = c % 2
            cb_ = c + D
            kb.need(sy, "cm%d" % par)
            ld(slh[2 * par][:, :], bass.AP(wexth, c * 512, [[1, 128], [1, 384]]), tag="cw%d" % par)
            ld(sll[2 * par][:, :], bass.AP(wextl, c * 512, [[1, 128], [1, 384]]), tag="cw%d" % par)
            ld(slh[2 * par + 1][:, :], bass.AP(wexth, cb_ * 512, [[1, 128], [1, 384]]), tag="cw%d" % par)
            ld(sll[2 * par + 1][:, :], bass.AP(wextl, cb_ * 512, [[1, 128], [1, 384]]), tag="cw%d" % par)
            kb.need(sc, "cm%d" % par)
            kb.mark(sc.dma_start_transpose(out=zsh[2 * par][:, :],
                    in_=zcmH[c, :].rearrange("(s p) -> s p", p=128)), "cz%d" % par, 16)
            kb.mark(sc.dma_start_transpose(out=zsl[2 * par][:, :],
                    in_=zcmL[c, :].rearrange("(s p) -> s p", p=128)), "cz%d" % par, 16)
            kb.mark(sc.dma_start_transpose(out=zsh[2 * par + 1][:, :],
                    in_=zcmH[cb_, :].rearrange("(s p) -> s p", p=128)), "cz%d" % par, 16)
            kb.mark(sc.dma_start_transpose(out=zsl[2 * par + 1][:, :],
                    in_=zcmL[cb_, :].rearrange("(s p) -> s p", p=128)), "cz%d" % par, 16)
            wdma(t, "cw%d" % par); wdma(t, "cz%d" % par)
            kb.need(t, "cg%d" % par)
            for side, ch_ in ((0, c), (1, cb_)):
                bk = 2 * par + side
                sslh, ssll = slh[2 * par + side], sll[2 * par + side]
                zh, zl = zsh[2 * par + side], zsl[2 * par + side]
                for a in range(3):
                    lhs_h = sslh[:, 128 * a:128 * a + 128]
                    lhs_l = ssll[:, 128 * a:128 * a + 128]
                    rh = bass.AP(zh[:, :].tensor, a, [[48, 128], [24, 2], [1, 16]])
                    rl = bass.AP(zl[:, :].tensor, a, [[48, 128], [24, 2], [1, 16]])
                    t.matmul(bank(bk, 32), lhs_h, rh, start=(a == 0), stop=False)
                    t.matmul(bank(bk, 32), lhs_h, rl, start=False, stop=False)
                    kb.mark(t.matmul(bank(bk, 32), lhs_l, rh, start=False,
                                     stop=(a == 2)), ("cm%d" % par) if (a == 2 and side == 1) else "mm")
            kb.need(v, "cm%d" % par)
            kb.mark(v.tensor_scalar(out=cq[:, 0:32], in0=bank(2 * par, 32),
                                    scalar1=bab[:, c:c + 1], scalar2=None, op0=ALU.add), "ev")
            kb.mark(v.tensor_scalar(out=cq[:, 32:64], in0=bank(2 * par + 1, 32),
                                    scalar1=bab[:, cb_:cb_ + 1], scalar2=None, op0=ALU.add), "ev")
            kb.need(v, "ev")
            kb.mark(v.nop(), "cg%d" % par)
            kb.need(sc, "ev"); kb.need(sc, "cgq")
            kb.mark(sc.activation(out=sq_[:, :], in_=cq[:, :], func=AF.Sigmoid), "act")
            kb.need(v, "act")
            kb.mark(v.tensor_tensor(out=gq[:, 0:32], in0=cq[:, 0:32], in1=sq_[:, 32:64], op=ALU.mult), "ev")
            kb.mark(v.tensor_tensor(out=gq[:, 32:64], in0=cq[:, 32:64], in1=sq_[:, 0:32], op=ALU.mult), "ev")
            kb.need(v, "ev")
            kb.mark(v.nop(), "cgq")
            kb.need(t, "ev"); kb.need(t, "ctr")
            kb.mark(t.transpose(PSUM[0:64, 3584:3712], gq[:, :], c_revf[:, :]), "mm")
            kb.need(v, "mm"); kb.need(v, "ctd")
            split(PSUM[0:64, 3584:3712], trsH[:, :], trsL[:, :], "ctr")
            kb.need(sc, "ctr")
            for trs, dst_ in ((trsH, gatH), (trsL, gatL)):
                dst_ap = bass.AP(dst_.tensor, c * NTOK, [[D * NTOK, 2], [2048, 2], [128, 16], [1, 128]])
                kb.mark(sc.dma_start(out=dst_ap, in_=trs[:, :]), "ctd", 16)
            kb.need(t, "ctd")
        kb.barrier()
        p10.close()

    if PHASE_LIMIT >= 11:
        p11 = ExitStack()
        wsH = sb("wsHb", [128, 8192], BF16, p11)
        wsL = sb("wsLb", [128, 8192], BF16, p11)
        ws2H = sb("ws2Hb", [128, 8192], BF16, p11)
        ws2L = sb("ws2Lb", [128, 8192], BF16, p11)
        bo1c = sb("bo1c", [128, 8], F32, p11)
        ld(bo1c[:, 0:8].rearrange("p (c g) -> p c g", c=8),
           dbo1[:, :].rearrange("(c p) g -> p c g", p=128), allow_slow_non_contiguous=True)
        for kc in range(8):
            ld(wsH[:, kc * 1024:(kc + 1) * 1024], dWo1[0][kc * 128:(kc + 1) * 128, :], tag="w")
            ld(wsL[:, kc * 1024:(kc + 1) * 1024], dWo1[1][kc * 128:(kc + 1) * 128, :], tag="w")
            ld(ws2H[:, kc * 1024:(kc + 1) * 1024], dWo1[0][(kc + 8) * 128:(kc + 9) * 128, :], tag="w")
            ld(ws2L[:, kc * 1024:(kc + 1) * 1024], dWo1[1][(kc + 8) * 128:(kc + 9) * 128, :], tag="w")
        wdma(t, "w"); wdma(v, DQ)
        for tt in range(NT512):
            L = tt * 512
            kb.tile_sync()
            for c in range(NCH):
                ld(hiB[c][:], gatH[c * 128:(c + 1) * 128, L:L + 512], tag="x")
                ld(loB[c][:], gatL[c * 128:(c + 1) * 128, L:L + 512], tag="x")
                ld(hi8[c][:], gatH[(c + 8) * 128:(c + 9) * 128, L:L + 512], tag="x")
                ld(lo8[c][:], gatL[(c + 8) * 128:(c + 9) * 128, L:L + 512], tag="x")
            wdma(t, "x")
            for oc in range(8):
                bk = oc % 4
                kb.need(t, "ev")
                for kc in range(8):
                    kb.mm3(bank(bk), wsH[:, kc * 1024 + oc * 128:kc * 1024 + (oc + 1) * 128],
                           wsL[:, kc * 1024 + oc * 128:kc * 1024 + (oc + 1) * 128],
                           hiB[kc][:], loB[kc][:], start=(kc == 0), stop=False)
                for kc in range(8):
                    kb.mm3(bank(bk), ws2H[:, kc * 1024 + oc * 128:kc * 1024 + (oc + 1) * 128],
                           ws2L[:, kc * 1024 + oc * 128:kc * 1024 + (oc + 1) * 128],
                           hi8[kc][:], lo8[kc][:], start=False, stop=(kc == 7),
                           mark=("mm" if kc == 7 else None))
                kb.need(v, "mm"); kb.need(v, "st")
                kb.mark(v.tensor_scalar(out=wt[3][:], in0=bank(bk), scalar1=bo1c[:, oc:oc + 1],
                                        scalar2=None, op0=ALU.add), "ev")
                kb.need(v, "ev"); kb.need(sc, "ev")
                st(s2d[oc * 128:(oc + 1) * 128, L:L + 512], wt[3][:], q=sc, tag="st")
                kb.need(v, "st")
        kb.barrier()
        p11.close()
        pA.close()

    if PHASE_LIMIT >= 12:
        p1213 = ExitStack()
        idx16a = sb("idx16a", [128, 16 * 32], I16, p1213)
        pf = sb("pf", [16, 16 * 32], F32, p1213)
        p12 = ExitStack()
        gw1h = sb("gw1h", [128, 64], BF16, p12)
        gw1l = sb("gw1l", [128, 64], BF16, p12)
        probsT = sb("probsT", [8, NTOK], F32, p12)
        pk128 = sb("pk128", [128, 16 * 16], F32, p12)
        pw16 = sb("pw16", [16, 16 * 128], F32, p12)
        th2 = sb("th2", [1, 32], F32, p12)
        thb = sb("thb", [16, 1], F32, p12)
        m_gt = sb("m_gt", [16, 128], F32, p12)
        meq = sb("meq", [16, 128], F32, p12)
        meqb = sb("meqb", [16, 128], BF16, p12)
        mgtb = sb("mgtb", [16, 128], BF16, p12)
        rsc = sb("rsc", [1, 128], F32, p12)
        rnk = sb("rnk", [16, 128], F32, p12)
        sel = sb("sel", [16, 128], F32, p12)
        tmpi = sb("tmpi", [16, 128], F32, p12)
        tmpp = sb("tmpp", [16, 128], F32, p12)
        zrow = sb("zrow", [1, 128], F32, p12)
        idxf = sb("idxf", [16, 16 * 32], F32, p12)
        numf = sb("numf", [1, 1], U32, p12)
        glt2 = sb("glt2", [128, 8], F32, p12)
        gsum = sb("gsum", [128, 4], F32, p12)

        ld(gw1h[:, :].rearrange("p (c e) -> p c e", c=8), dgw1[0][:, :].rearrange("(c p) e -> p c e", p=128))
        ld(gw1l[:, :].rearrange("p (c e) -> p c e", c=8), dgw1[1][:, :].rearrange("(c p) e -> p c e", p=128))
        kb.mark(v.memset(zrow[:, :], 0.0), "ev")
        for tt in range(NT512):
            L = tt * 512
            kb.tile_sync()
            for c in range(NCH):
                ld(hiB[c][:], xn2H[c * 128:(c + 1) * 128, L:L + 512], tag="x")
                ld(loB[c][:], xn2L[c * 128:(c + 1) * 128, L:L + 512], tag="x")
            wdma(t, "x")
            for kc in range(8):
                kb.mm3(PSUM[0:8, 0:512], gw1h[:, kc * 8:(kc + 1) * 8], gw1l[:, kc * 8:(kc + 1) * 8],
                       hiB[kc][:], loB[kc][:], start=(kc == 0), stop=(kc == 7),
                       mark=("mm" if kc == 7 else None))
            kb.need(v, "mm")
            kb.mark(v.tensor_copy(out=wt[0][0:8, :], in_=PSUM[0:8, 0:512]), "ev")
            kb.need(t, "ev")
            for j in range(4):
                kb.need(t, "ev")
                kb.mark(t.transpose(PSUM[0:128, 512:520], wt[0][0:8, j * 128:(j + 1) * 128],
                                    c_identf[0:8, 0:8]), "mm")
                kb.need(v, "mm")
                kb.mark(v.tensor_reduce(out=gsum[:, 0:1], in_=PSUM[0:128, 512:520], axis=AXL.X,
                                        op=ALU.max), "ev")
                kb.need(v, "ev")
                kb.mark(v.tensor_scalar(out=glt2[:, :], in0=PSUM[0:128, 512:520], scalar1=gsum[:, 0:1],
                                        scalar2=None, op0=ALU.subtract), "ev")
                kb.need(sc, "ev")
                kb.mark(sc.activation(out=glt2[:, :], in_=glt2[:, :], func=AF.Exp,
                                      accum_out=gsum[:, 1:2]), "act")
                kb.need(v, "act")
                kb.mark(v.nop(), "evt")
                kb.mark(v.reciprocal(out=gsum[:, 2:3], in_=gsum[:, 1:2]), "ev")
                kb.need(v, "ev")
                kb.mark(v.tensor_scalar(out=glt2[:, :], in0=glt2[:, :], scalar1=gsum[:, 2:3],
                                        scalar2=None, op0=ALU.mult), "ev")
                kb.need(t, "ev"); kb.need(t, "evt")
                kb.mark(t.transpose(PSUM[0:8, 1024:1152], glt2[:, :], c_identf[0:128, 0:128]), "mm")
                kb.need(v, "mm")
                kb.mark(v.tensor_copy(out=probsT[:, L + j * 128:L + (j + 1) * 128],
                                      in_=PSUM[0:8, 1024:1152]), "ev")
                kb.need(v, "ev")
        gp.load_library(LC.attn)
        for pr in range(16):
            b, e = pr // 8, pr % 8
            row = probsT[e:e + 1, b * T:(b + 1) * T]
            kb.need(sy, "ev")
            ld(pk128[:, pr * 16:(pr + 1) * 16], bass.AP(row.tensor, row.offset, [[NTOK, 1], [16, 128], [1, 16]]),
               tag="x")
        wdma(t, "x"); wdma(gp, "x"); wdma(v, "x")
        for pr in range(16):
            kb.need(t, "ev")
            kb.mark(t.transpose(PSUM[0:16, 0:128], pk128[:, pr * 16:(pr + 1) * 16],
                                c_identf[0:128, 0:128]), "mm")
            kb.need(v, "mm")
            kb.mark(v.tensor_copy(out=pw16[:, pr * 128:(pr + 1) * 128], in_=PSUM[0:16, 0:128]), "ev")
            kb.need(v, "ev")
        for pr in range(16):
            kb.need(gp, "ev"); kb.need(gp, "mm")
            kb.mark(gp.kth_largest(th2[0:1, pr * 2:pr * 2 + 2], pk128[:, pr * 16:(pr + 1) * 16],
                                   16, 510, quantile=0.5), "gb")
            kb.need(gp, "gb")
            kb.mark(gp.partition_broadcast(thb[:, :], th2[0:1, pr * 2 + 1:pr * 2 + 2]), "gb")
            kb.need(v, "gb")
            pw = pw16[:, pr * 128:(pr + 1) * 128]
            kb.mark(v.tensor_scalar(out=m_gt[:, :], in0=pw, scalar1=thb[:, 0:1], scalar2=None,
                                    op0=ALU.is_gt), "ev")
            kb.mark(v.tensor_scalar(out=meq[:, :], in0=pw, scalar1=thb[:, 0:1], scalar2=None,
                                    op0=ALU.is_equal), "ev")
            kb.need(v, "ev")
            kb.mark(v.tensor_copy(out=meqb[:, :], in_=meq[:, :]), "ev")
            kb.mark(v.tensor_copy(out=mgtb[:, :], in_=m_gt[:, :]), "ev")
            kb.need(t, "ev")
            kb.mark(t.matmul(PSUM[0:1, 0:128], c_ones[0:16, 0:1], meqb[:, :], start=True, stop=True), "mm")
            kb.mark(t.matmul(PSUM[0:1, 512:640], c_ones[0:16, 0:1], mgtb[:, :], start=True, stop=True), "mm")
            kb.mark(t.matmul(PSUM[0:16, 1024:1152], c_tri16[:, :], meqb[:, :], start=True, stop=True), "mm")
            kb.need(v, "mm")
            kb.mark(v.tensor_tensor_scan(out=rsc[:, :], data0=zrow[:, :], data1=PSUM[0:1, 0:128],
                                         initial=0.0, op0=ALU.add, op1=ALU.add), "ev")
            kb.need(v, "ev")
            kb.mark(v.tensor_tensor(out=rsc[:, :], in0=rsc[:, :], in1=PSUM[0:1, 0:128],
                                    op=ALU.subtract), "ev")
            kb.mark(v.tensor_reduce(out=th2[0:1, 32:33], in_=PSUM[0:1, 512:640], axis=AXL.X,
                                    op=ALU.add), "ev") if False else None
            kb.mark(v.tensor_reduce(out=rnk[0:1, 0:1], in_=PSUM[0:1, 512:640], axis=AXL.X,
                                    op=ALU.add), "ev")
            kb.need(v, "ev")
            kb.mark(v.tensor_scalar(out=rnk[0:1, 0:1], in0=rnk[0:1, 0:1], scalar1=-1.0,
                                    scalar2=float(CAP), op0=ALU.mult, op1=ALU.add), "ev")
            kb.need(gp, "ev")
            kb.mark(gp.partition_broadcast(thb[:, :], rnk[0:1, 0:1]), "gb")
            kb.need(gp, "gb")
            kb.mark(gp.partition_broadcast(rnk[:, :], rsc[0:1, :]), "gb")
            kb.need(v, "gb")
            kb.mark(v.tensor_tensor(out=rnk[:, :], in0=rnk[:, :], in1=PSUM[0:16, 1024:1152],
                                    op=ALU.add), "ev")
            kb.need(v, "ev")
            kb.mark(v.tensor_scalar(out=rnk[:, :], in0=rnk[:, :], scalar1=thb[:, 0:1], scalar2=None,
                                    op0=ALU.is_lt), "ev")
            kb.need(v, "ev")
            kb.mark(v.tensor_tensor(out=sel[:, :], in0=meq[:, :], in1=rnk[:, :], op=ALU.mult), "ev")
            kb.need(v, "ev")
            kb.mark(v.tensor_tensor(out=sel[:, :], in0=sel[:, :], in1=m_gt[:, :], op=ALU.add), "ev")
            kb.need(v, "ev")
            kb.mark(v.memset(tmpi[:, :], -1.0), "ev")
            kb.mark(v.memset(tmpp[:, :], -1.0), "ev")
            kb.need(v, "ev")
            kb.mark(v.copy_predicated(out=tmpi[:, :], mask=sel[:, :].bitcast(mybir.dt.int32),
                                      data=c_iota16[:, :]), "ev")
            kb.mark(v.copy_predicated(out=tmpp[:, :], mask=sel[:, :].bitcast(mybir.dt.int32),
                                      data=pw), "ev")
            kb.need(v, "ev")
            gp.load_library(LC.sparse_gather)
            kb.need(gp, "ev")
            kb.mark(gp.sparse_gather(idxf[:, pr * 32:(pr + 1) * 32], tmpi[:, :], num_found=numf[:, :]), "gb")
            kb.need(gp, "gb")
            kb.mark(gp.sparse_gather(pf[:, pr * 32:(pr + 1) * 32], tmpp[:, :], num_found=numf[:, :]), "gb")
            gp.load_library(LC.attn)
            kb.need(v, "gb")
            kb.mark(v.tensor_copy(out=idx16a[0:16, pr * 32:(pr + 1) * 32],
                                  in_=idxf[:, pr * 32:(pr + 1) * 32]), "ev")
            kb.need(v, "ev")
        for sz in (16, 32, 64):
            kb.need(sy, "ev"); wdma(sy, "tk0")
            ld(idx16a[sz:2 * sz, :], idx16a[0:sz, :], tag="tk0")
        gp.load_library(LC.mlp)
        kb.barrier()
        p12.close()

    if PHASE_LIMIT >= 13:
        p13 = ExitStack()
        w1s = [(sb("w1sh%d" % i, [128, 1024], BF16, p13), sb("w1sl%d" % i, [128, 1024], BF16, p13))
               for i in range(2)]
        _w2s0 = (sb("w2sh0", [128, 4096], BF16, p13), sb("w2sl0", [128, 4096], BF16, p13))
        w2s = [_w2s0, _w2s0]
        xinH = [sb("xinH%d" % b, [128, NE * 512], BF16, p13) for b in range(BPC)]
        xinL = [sb("xinL%d" % b, [128, NE * 512], BF16, p13) for b in range(BPC)]
        hfh = sb("hfh", [128, 32 * 512], BF16, p13)
        hfl = sb("hfl", [128, 32 * 512], BF16, p13)
        h32 = wt[0]
        pay = sb("pay", [128, 4 * 1024], F32, p13)
        prow = rowD
        pb512 = bcA
        y32 = wt[1]
        nreg = gp.alloc_register()
        gp.reg_mov(nreg, CAP)
        kb.mark(v.memset(pay[:, :], 0.0), "ev")
        kb.need(sy, "ev")
        for b in range(BPC):
            for par2 in range(2):
                for tk in range(T // 128):
                    st(acc1[b, par2, tk * 128:(tk + 1) * 128, :], pay[:, 0:1024], tag="st")
        wdma(sy, "st"); kb.need(v, "st")
        for e in range(NE):
            for b in range(BPC):
                pr = b * 8 + e
                kb.need(gp, "ev"); kb.need(gp, "mm"); kb.need(gp, "hsp")
                kb.mark(gp.dma_gather(xinH[b][:, :].rearrange("p (c s) -> p c s", c=8),
                                      xt2H[b][:, :], idx16a[:, pr * 32:(pr + 1) * 32],
                                      CAP, nreg, D, transpose=True), "dg", 16)
                kb.mark(gp.dma_gather(xinL[b][:, :].rearrange("p (c s) -> p c s", c=8),
                                      xt2L[b][:, :], idx16a[:, pr * 32:(pr + 1) * 32],
                                      CAP, nreg, D, transpose=True), "dg", 16)
            wdma(t, "dg")
            for b in range(BPC):
                pr = b * 8 + e
                kb.need(sc, "ev")
                st(pbd[:, :], pf[:, pr * 32:(pr + 1) * 32], q=sc, tag="tk1")
                wdma(sy, "tk1")
                ld(prow[0:1, 0:512], pbd.rearrange("r c -> c r"), tag="tk2", allow_slow_non_contiguous=True)
                wdma(gp, "tk2")
                kb.need(gp, "pay")
                kb.mark(gp.partition_broadcast(pb512[:, :], prow[0:1, 0:512]), "gb")
                for oc in range(32):
                    sl1 = w1s[oc % 2]
                    if True:
                        kb.need(sy, "mm")
                        ld(sl1[0][:, :].rearrange("p (kc m) -> p kc m", kc=8),
                           dW1[0][e, :, oc * 128:(oc + 1) * 128].rearrange("(kc p) m -> p kc m", p=128),
                           tag="w1l%d" % (oc % 2))
                        ld(sl1[1][:, :].rearrange("p (kc m) -> p kc m", kc=8),
                           dW1[1][e, :, oc * 128:(oc + 1) * 128].rearrange("(kc p) m -> p kc m", p=128),
                           tag="w1l%d" % (oc % 2))
                    wdma(t, "w1l%d" % (oc % 2))
                    bk1 = oc % 2
                    kb.need(t, "act")
                    for kc in range(8):
                        kb.mm3(PSUM[0:128, bk1 * 512:bk1 * 512 + 512],
                               sl1[0][:, kc * 128:(kc + 1) * 128], sl1[1][:, kc * 128:(kc + 1) * 128],
                               xinH[b][:, kc * 512:(kc + 1) * 512], xinL[b][:, kc * 512:(kc + 1) * 512],
                               start=(kc == 0), stop=(kc == 7), mark=("mm" if kc == 7 else None))
                    kb.need(sc, "mm"); kb.need(sc, "hsp")
                    kb.mark(sc.activation(out=h32[:, :], in_=PSUM[0:128, bk1 * 512:bk1 * 512 + 512],
                                          func=AF.Gelu), "act")
                    kb.need(v, "act")
                    split(h32[:, :], hfh[:, oc * 512:(oc + 1) * 512], hfl[:, oc * 512:(oc + 1) * 512], "hsp")
                kb.need(t, "hsp")
                for oc2 in range(8):
                    sl2 = w2s[oc2 % 2]
                    if True:
                        kb.need(sc, "w2m")
                        ld(sl2[0][:, :].rearrange("p (kc m) -> p kc m", kc=32),
                           dW2[0][e, :, oc2 * 128:(oc2 + 1) * 128].rearrange("(kc p) m -> p kc m", p=128),
                           q=sc, tag="w2l%d" % (oc2 % 2))
                        ld(sl2[1][:, :].rearrange("p (kc m) -> p kc m", kc=32),
                           dW2[1][e, :, oc2 * 128:(oc2 + 1) * 128].rearrange("(kc p) m -> p kc m", p=128),
                           q=sc, tag="w2l%d" % (oc2 % 2))
                    wdma(t, "w2l%d" % (oc2 % 2))
                    bk2 = 2 + (oc2 % 2)
                    kb.need(t, "ev")
                    for kc2 in range(32):
                        kb.mm3(PSUM[0:128, bk2 * 512:bk2 * 512 + 512],
                               sl2[0][:, kc2 * 128:(kc2 + 1) * 128], sl2[1][:, kc2 * 128:(kc2 + 1) * 128],
                               hfh[:, kc2 * 512:(kc2 + 1) * 512], hfl[:, kc2 * 512:(kc2 + 1) * 512],
                               start=(kc2 == 0), stop=(kc2 == 31), mark=("w2m" if kc2 == 31 else None))
                    kb.need(v, "w2m"); kb.need(v, "gb"); kb.need(v, "pay")
                    for b2_ in range(BPC):
                        for p2_ in range(2):
                            kb.need(v, "sca%d%d" % (b2_, p2_))
                    kb.mark(v.tensor_tensor(out=y32[:, :], in0=PSUM[0:128, bk2 * 512:bk2 * 512 + 512],
                                            in1=pb512[:, :], op=ALU.mult), "ev")
                    kb.need(v, "ev"); kb.need(t, "ev")
                    for sb2 in range(4):
                        kb.mark(t.transpose(PSUM[0:128, 3584:3712], y32[:, sb2 * 128:(sb2 + 1) * 128],
                                            c_identf[:, :]), "trm")
                        kb.need(v, "trm")
                        kb.mark(v.tensor_copy(out=pay[:, sb2 * 1024 + oc2 * 128:sb2 * 1024 + (oc2 + 1) * 128],
                                              in_=PSUM[0:128, 3584:3712]), "pay")
                        kb.need(v, "pay"); kb.need(t, "pay")
                kb.need(gp, "pay"); kb.need(gp, "sca%d%d" % (b, e % 2))
                kb.mark(gp.dma_scatter_add(acc1[b, e % 2, :, :], pay[:, :].rearrange("p (s q) -> p s q", s=4),
                                           idx16a[:, pr * 32:(pr + 1) * 32], CAP, nreg, D),
                        "sca%d%d" % (b, e % 2), 16)
        kb.barrier()
        p13.close()
        p1213.close()

    if PHASE_LIMIT >= 14:
        p14 = ExitStack()
        a32 = sb("a32", [128, 128], F32, p14)
        b32 = sb("b32", [128, 128], F32, p14)
        g2t = sb("g2t", [128, 512], F32, p14)
        for tt in range(NT512):
            L = tt * 512
            b = tt // 4
            t0_ = (tt % 4) * 512
            for c in range(NCH):
                kb.tile_sync()
                for blk in range(4):
                    trng = slice(t0_ + blk * 128, t0_ + (blk + 1) * 128)
                    ld(a32[:, :], acc1[b, 0, trng, c * 128:(c + 1) * 128], tag="x")
                    ld(b32[:, :], acc1[b, 1, trng, c * 128:(c + 1) * 128], tag="x")
                    wdma(v, "x")
                    kb.mark(v.tensor_tensor(out=a32[:, :], in0=a32[:, :], in1=b32[:, :], op=ALU.add), "ev")
                    kb.need(sc, "ev")
                    kb.mark(sc.activation(out=a32[:, :], in_=a32[:, :], func=AF.Silu), "act")
                    kb.need(t, "act"); kb.need(t, "ev")
                    kb.mark(t.transpose(PSUM[0:128, 0:128], a32[:, :], c_identf[:, :]), "mm")
                    kb.need(v, "mm")
                    kb.mark(v.tensor_copy(out=g2t[:, blk * 128:(blk + 1) * 128], in_=PSUM[0:128, 0:128]), "ev")
                    kb.need(v, "ev"); kb.need(sy, "ev")
                ld(wt[0][:], s2d[c * 128:(c + 1) * 128, L:L + 512], tag="x")
                ld(wt[1][:], xBd[c * 128:(c + 1) * 128, L:L + 512], tag="x")
                wdma(v, "x")
                kb.mark(v.tensor_tensor(out=g2t[:, :], in0=g2t[:, :], in1=wt[0][:], op=ALU.mult), "ev")
                kb.need(v, "ev")
                kb.mark(v.tensor_tensor(out=g2t[:, :], in0=g2t[:, :], in1=wt[1][:], op=ALU.add), "ev")
                kb.need(v, "ev"); kb.need(sc, "ev")
                st(out_x[c * 128:(c + 1) * 128, L:L + 512], g2t[:, :], q=sc, tag="st")
        kb.barrier()
        p14.close()

    kb.barrier()
    return kb, locals()


def kernel(**inputs):
    kb, env = build_program(inputs)
    mybir.codegen_inst_isa_subclasses(kb.nc)
    xfull = np.asarray(inputs["L0_attn_x"], dtype=np.float32)
    in_maps = []
    for c in range(NCORES):
        m = dict(kb.inputs)
        shard = xfull[c * BPC:(c + 1) * BPC]
        m["x"] = np.ascontiguousarray(
            shard[:, :, PERM].transpose(2, 0, 1).reshape(D, NTOK).astype(np.float32))
        in_maps.append(m)
    res = run_bass_kernel_spmd(kb.nc, in_maps, core_ids=list(range(NCORES)))
    outs = []
    for c in range(NCORES):
        o = np.asarray(res.results[c]["out"])
        o = o.reshape(D, BPC, T).transpose(1, 2, 0)[:, :, IPERM]
        outs.append(o)
    return np.ascontiguousarray(np.concatenate(outs, axis=0), dtype=np.float32)
